# version 03 (final)
# baseline (speedup 1.0000x reference)
import numpy as np
from contextlib import ExitStack

import concourse.bass as bass
import concourse.mybir as mybir
from concourse.bass_utils import run_bass_kernel_spmd

F32 = mybir.dt.float32
BF16 = mybir.dt.bfloat16
AF = mybir.ActivationFunctionType
ALU = mybir.AluOpType

P = 128
D = 1024
DI = 2048
DH = 4096
DS = 16
DTR = 64
L = 4096
T = 512
TH = 256
NT = L // T
CD = D // P
CI = DI // P
CH = DH // P
LN_EPS = 1e-5
RMS_EPS = 1.1920929e-07
NAB = 4
NYP = 8

ENGS = ("TE", "VE", "SC", "GP", "SY")


class Bld:

    def __init__(self):
        self.steps = []
        self.counts = {}
        self.seg = "s"

    def step(self, eng, deps, fn, inc=True, n_dma=0, ch=None):
        if n_dma:
            key = ("DMA", ch if ch is not None else f"{eng}_{self.seg}")
        else:
            key = (eng, self.seg)
        deps = [d for d in deps if d is not None]
        self.steps.append((eng, key, deps, fn, inc, n_dma))
        if key not in self.counts:
            self.counts[key] = 0
        if n_dma:
            self.counts[key] += 16 * n_dma
        elif inc:
            self.counts[key] += 1
        return (key, self.counts[key])


def _emit(nc, b, es):
    sems = {k: es.enter_context(nc.semaphore(f"S_{k[0]}_{k[1]}"))
            for k in b.counts}
    blk = es.enter_context(nc.Block())

    def body_for(ename):
        def body(eng):
            last_wait = {}
            for (e, key, deps, fn, inc, n_dma) in b.steps:
                if e != ename:
                    continue
                for (dkey, dval) in deps:
                    if dkey[0] == ename:
                        continue
                    if dval > last_wait.get(dkey, 0):
                        eng.wait_ge(sems[dkey], dval)
                        last_wait[dkey] = dval
                if n_dma:
                    fn(eng, sems[key])
                else:
                    inst = fn(eng)
                    if inc:
                        inst.then_inc(sems[key], 1)
        return body

    blk.tensor(body_for("TE"))
    blk.vector(body_for("VE"))
    blk.scalar(body_for("SC"))
    blk.gpsimd(body_for("GP"))
    blk.sync(body_for("SY"))


def ceil_div(a, bb):
    return (a + bb - 1) // bb


def build_program(n_tiles=NT, debug_taps=()):
    nc = bass.Bass("TRN2", target_bir_lowering=False,
                   detect_race_conditions=False)

    def din(name, shape, dt=F32):
        return nc.dram_tensor(name, list(shape), dt, kind="ExternalInput").ap()

    x_d = din("x", (P, CD * L))
    out_d = nc.dram_tensor("out", [P, CD * L], F32, kind="ExternalOutput").ap()

    wAT = din("wAT", (D, 2 * DH))
    h1T = din("h1T", (DH, DH))
    h2T = din("h2T", (DH, DH))
    w3T = din("w3T", (DH, D))
    inT = din("inT", (2, D, 2 * DI))
    outT = din("outT", (2, DI, D))
    mlp1T = din("mlp1T", (D, 2 * D))
    mlp2T = din("mlp2T", (2 * D, D))
    dtpT = din("dtpT", (2, DTR, DI))
    dtrT = din("dtrT", (2, DI, DTR))
    BT = din("BT", (2, DI, DS))
    CT = din("CT", (2, DI, DS))
    alog_d = din("alog", (P, 2 * CI * DS))
    convw_d = din("convw", (P, 2 * 4 * CI))
    convb_d = din("convb", (P, 2 * CI))
    dtpb_d = din("dtpb", (P, 2 * CI))
    Dp_d = din("Dp", (P, 2 * CI))
    mlpb1_d = din("mlpb1", (P, DI // P))
    mlpb2_d = din("mlpb2", (P, CD))
    lnw_d = din("lnw", (P, CD))
    ln2w_d = din("ln2w", (P, CD))
    s1_d = din("s1", (P, CH))
    s2_d = din("s2", (P, CH))
    h1s_d = din("h1s", (P, CH))
    h2s_d = din("h2s", (P, CH))
    s3_d = din("s3", (P, CD))
    selp_d = din("selp", (DS, DS * P))

    def dint(name, shape):
        return nc.dram_tensor(name, list(shape), BF16, kind="Internal").ap()

    wAT_b = dint("wAT_b", (D, 2 * DH))
    h1T_b = dint("h1T_b", (DH, DH))
    h2T_b = dint("h2T_b", (DH, DH))
    w3T_b = dint("w3T_b", (DH, D))
    inT_b = dint("inT_b", (2, D, 2 * DI))
    outT_b = dint("outT_b", (2, DI, D))
    mlp1T_b = dint("mlp1T_b", (D, 2 * D))
    mlp2T_b = dint("mlp2T_b", (2 * D, D))
    dtpT_b = dint("dtpT_b", (2, DTR, DI))
    dtrT_b = dint("dtrT_b", (2, DI, DTR))
    BT_b = dint("BT_b", (2, DI, DS))
    CT_b = dint("CT_b", (2, DI, DS))

    b = Bld()
    es = ExitStack()
    _uid = [0]
    with es:
        with nc.sbuf_tensor("wipe", [128, 53000], F32) as wipe_t:
            t_wipe = b.step("VE", [],
                            (lambda w: lambda v: v.memset(w[:], 0.0))(wipe_t))
        def S_(stk, name, shape, dt=F32):
            _uid[0] += 1
            return stk.enter_context(
                nc.sbuf_tensor(f"{name}_{_uid[0]}", list(shape), dt))

        def S(name, shape, dt=F32):
            return S_(es, name, shape, dt)

        A_cols = S("A_cols", (P, 2 * CI * DS))
        convw_c = S("convw_c", (P, 2 * 4 * CI))
        convb_c = S("convb_c", (P, 2 * CI))
        dtpb_c = S("dtpb_c", (P, 2 * CI))
        Dp_c = S("Dp_c", (P, 2 * CI))
        mlpb1_c = S("mlpb1_c", (P, DI // P))
        mlpb2_c = S("mlpb2_c", (P, CD))
        lnw_c = S("lnw_c", (P, CD))
        ln2w_c = S("ln2w_c", (P, CD))
        s1_c = S("s1_c", (P, CH))
        s2_c = S("s2_c", (P, CH))
        h1s_c = S("h1s_c", (P, CH))
        h2s_c = S("h2s_c", (P, CH))
        s3h_c = S("s3h_c", (P, CD))
        selp_b = S("selp_b", (DS, DS * P), BF16)
        ones_col = S("ones_col", (P, 1))
        ones_row = S("ones_row", (1, P))
        H_state = S("H_state", (P, 2 * CI * DS))
        tails = S("tails", (P, 2 * CI * 3))
        dtr_sb = S("dtr_sb", (P, 2 * CI * DTR), BF16)
        Bw_sb = S("Bw_sb", (P, 2 * CI * DS), BF16)
        Cw_sb = S("Cw_sb", (P, 2 * CI * DS), BF16)
        row_a = S("row_a", (1, T))
        row_b = S("row_b", (1, T))
        eps1_c = S("eps1_c", (P, 1))
        eps2_c = S("eps2_c", (P, 1))
        tt1 = S("tt1", (P, T))
        tt2 = S("tt2", (P, T))
        wbuf = [S(f"wbuf{i}", (P, 8 * P), BF16) for i in range(2)]

        x_t = S("x_t", (P, CD * T))
        x1_t = S("x1_t", (P, CD * T))
        u0_t = S("u0_t", (P, CD * T))
        ua_t = S("ua_t", (P, CD * T))
        nb_t = S("nb_t", (P, CD * T), BF16)
        nb2_t = S("nb2_t", (P, CD * T), BF16)

        ps = [es.enter_context(nc.psum_tensor(f"ps{i}", [P, T], F32))
              for i in range(8)]

        def ld(dst_ap, src_ap):
            def f(eng, sem):
                eng.dma_start(out=dst_ap, in_=src_ap).then_inc(sem, 16)
            return f

        wb_idx = [0]
        wb_tok = [None, None]
        t_ld = None
        for dst, src in [
            (convw_c[:], convw_d), (convb_c[:], convb_d),
            (dtpb_c[:], dtpb_d), (Dp_c[:], Dp_d),
            (mlpb1_c[:], mlpb1_d), (mlpb2_c[:], mlpb2_d),
            (lnw_c[:], lnw_d), (ln2w_c[:], ln2w_d),
            (s1_c[:], s1_d), (s2_c[:], s2_d),
            (h1s_c[:], h1s_d), (h2s_c[:], h2s_d),
            (A_cols[:], alog_d), (s3h_c[:], s3_d),
        ]:
            t_ld = b.step("SY", [t_wipe], ld(dst, src), n_dma=1, ch="const")
        t_selp_raw = b.step("SY", [], ld(tt1[0:DS, 0:T], selp_d[:, 0:T]),
                            n_dma=1, ch="const")
        t_selp_raw = b.step("SY", [], ld(tt2[0:DS, 0:T], selp_d[:, T:2 * T]),
                            n_dma=1, ch="const")
        t_const_all = t_selp_raw

        def f_selp(v):
            v.tensor_copy(out=selp_b[:, 0:T], in_=tt1[0:DS, 0:T])
            return v.tensor_copy(out=selp_b[:, T:2 * T], in_=tt2[0:DS, 0:T])
        t_selp = b.step("VE", [t_selp_raw], f_selp)
        t_selp_raw2 = b.step("SY", [t_selp], ld(tt1[0:DS, 0:T],
                                                selp_d[:, 2 * T:3 * T]),
                             n_dma=1, ch="const2")
        t_selp_raw2 = b.step("SY", [], ld(tt2[0:DS, 0:T],
                                          selp_d[:, 3 * T:4 * T]),
                             n_dma=1, ch="const2")

        def f_selp2(v):
            v.tensor_copy(out=selp_b[:, 2 * T:3 * T], in_=tt1[0:DS, 0:T])
            return v.tensor_copy(out=selp_b[:, 3 * T:4 * T], in_=tt2[0:DS, 0:T])
        t_selp = b.step("VE", [t_selp_raw2], f_selp2)
        b.step("VE", [], lambda v: v.memset(ones_col[:], 1.0), inc=False)
        b.step("VE", [], lambda v: v.memset(eps1_c[:], LN_EPS), inc=False)
        t_eps = b.step("VE", [], lambda v: v.memset(eps2_c[:], RMS_EPS))
        b.step("VE", [], lambda v: v.memset(ones_row[:], 1.0), inc=False)
        b.step("VE", [], lambda v: v.memset(H_state[:], 0.0), inc=False)
        b.step("VE", [], lambda v: v.memset(tails[:], 0.0), inc=False)
        t_a1 = b.step("SC", [t_const_all], lambda s: s.activation(
            out=A_cols[:], in_=A_cols[:], func=AF.Exp))
        t_a2 = b.step("VE", [t_a1], lambda v: v.tensor_scalar_mul(
            A_cols[:], A_cols[:], -1.0))
        t_s3 = b.step("VE", [t_const_all], lambda v: v.tensor_scalar_mul(
            s3h_c[:], s3h_c[:], 0.5))

        CW = 2048
        with ExitStack() as cvs:
            cvt = [S_(cvs, f"cvt{i}", (P, CW)) for i in range(2)]
            cvtb = [S_(cvs, f"cvtb{i}", (P, CW), BF16) for i in range(2)]
            cvt_idx = [0]
            cvt_tok = [None, None]

            def convert(df, db, R, C):
                for i in range(ceil_div(R, P)):
                    r0 = i * P
                    pp = min(P, R - r0)
                    for c0 in range(0, C, CW):
                        cc = min(CW, C - c0)
                        bi = cvt_idx[0] % 2
                        cvt_idx[0] += 1
                        t_in = b.step("SY", [cvt_tok[bi]],
                                      ld(cvt[bi][0:pp, 0:cc],
                                         df[r0:r0 + pp, c0:c0 + cc]),
                                      n_dma=1, ch=f"cvi{bi}")

                        def fcast(v, bi=bi, pp=pp, cc=cc):
                            return v.tensor_copy(out=cvtb[bi][0:pp, 0:cc],
                                                 in_=cvt[bi][0:pp, 0:cc])
                        t_cast = b.step("VE", [t_in], fcast)
                        t_out = b.step("SY", [t_cast],
                                       ld(db[r0:r0 + pp, c0:c0 + cc],
                                          cvtb[bi][0:pp, 0:cc]),
                                       n_dma=1, ch=f"cvo{bi}")
                        cvt_tok[bi] = t_out

            convert(wAT, wAT_b, D, 2 * DH)
            convert(h1T, h1T_b, DH, DH)
            convert(h2T, h2T_b, DH, DH)
            convert(w3T, w3T_b, DH, D)
            convert(mlp1T, mlp1T_b, D, 2 * D)
            convert(mlp2T, mlp2T_b, 2 * D, D)
            for l in range(2):
                convert(inT[l], inT_b[l], D, 2 * DI)
                convert(outT[l], outT_b[l], DI, D)
                convert(dtpT[l], dtpT_b[l], DTR, DI)
                convert(dtrT[l], dtrT_b[l], DI, DTR)
                convert(BT[l], BT_b[l], DI, DS)
                convert(CT[l], CT_b[l], DI, DS)
            cvt_done = [cvt_tok[0], cvt_tok[1]]

        def f_small(eng, sem):
            for l in range(2):
                for ki in range(CI):
                    eng.dma_start(
                        out=dtr_sb[:, (l * CI + ki) * DTR:
                                   (l * CI + ki + 1) * DTR],
                        in_=dtrT_b[l][ki * P:(ki + 1) * P, :]).then_inc(sem, 16)
                    eng.dma_start(
                        out=Bw_sb[:, (l * CI + ki) * DS:(l * CI + ki + 1) * DS],
                        in_=BT_b[l][ki * P:(ki + 1) * P, :]).then_inc(sem, 16)
                    eng.dma_start(
                        out=Cw_sb[:, (l * CI + ki) * DS:(l * CI + ki + 1) * DS],
                        in_=CT_b[l][ki * P:(ki + 1) * P, :]).then_inc(sem, 16)
        t_small = b.step("SY", cvt_done, f_small, n_dma=2 * 3 * CI,
                         ch="small")

        start_toks = cvt_done + [t_small, t_a2, t_s3, t_selp, t_eps, t_const_all]
        wb_tok[0] = t_small
        wb_tok[1] = t_small


        def sc_act(dst, src, func, scale=1.0, bias=0.0):
            def f(s):
                return s.activation(out=dst, in_=src, func=func,
                                    scale=scale, bias=bias)
            return f

        def mm_phase(Kch, kp, Mch, wdram, rhs_fn, evac_fn, deps, N=T):
            group = 8
            out_toks = []
            prev_evacs = []
            for g0 in range(0, Mch, group):
                gm = min(group, Mch - g0)
                t_te = None
                for ki in range(Kch):
                    bi = wb_idx[0] % 2
                    wb_idx[0] += 1
                    t_w = b.step("SY", [wb_tok[bi]],
                                 ld(wbuf[bi][0:kp, 0:gm * P],
                                    wdram[ki * kp:ki * kp + kp,
                                          g0 * P:(g0 + gm) * P]),
                                 n_dma=1, ch=f"wq{bi}")

                    def fmm(v, bi=bi, ki=ki, gm=gm):
                        last = None
                        rhs = rhs_fn(ki)
                        for mi in range(gm):
                            last = v.matmul(ps[mi][:, 0:N],
                                            wbuf[bi][0:kp, mi * P:(mi + 1) * P],
                                            rhs, start=(ki == 0),
                                            stop=(ki == Kch - 1))
                        return last
                    dd = list(deps) + [t_w]
                    if ki == 0:
                        dd += prev_evacs
                    t_te = b.step("TE", dd, fmm)
                    wb_tok[bi] = t_te
                prev_evacs = []
                for mi in range(gm):
                    tok = evac_fn(g0 + mi, ps[mi][:, 0:N], [t_te])
                    prev_evacs.append(tok)
                    out_toks.append(tok)
            return out_toks

        def ph_norm_stats(src_fn, nch, width, deps, want_mean):
            toks = list(deps)
            last_sq = None
            for c in range(nch):
                def f_cp(v, c=c):
                    return v.tensor_copy(out=tt1[:, 0:width], in_=src_fn(c))
                t_cp = b.step("VE", toks + [last_sq], f_cp)
                t_sq = b.step("SC", [t_cp], sc_act(tt2[:, 0:width],
                                                   tt1[:, 0:width], AF.Square))

                def f_mm(v, c=c):
                    if want_mean:
                        v.matmul(ps[6][0:1, 0:width], ones_col[:],
                                 tt1[:, 0:width], start=(c == 0),
                                 stop=(c == nch - 1))
                    return v.matmul(ps[7][0:1, 0:width], ones_col[:],
                                    tt2[:, 0:width], start=(c == 0),
                                    stop=(c == nch - 1))
                last_sq = b.step("TE", [t_sq], f_mm)
                toks = []
            return last_sq

        def ph_ln(src_t, dst_f32, dst_b16, deps):
            t_mm = ph_norm_stats(lambda c: src_t[:, c * T:(c + 1) * T],
                                 CD, T, deps, True)

            def f_rows(v):
                v.tensor_scalar_mul(row_a[0:1, :], ps[6][0:1, 0:T], 1.0 / D)
                v.tensor_scalar_mul(row_b[0:1, :], ps[7][0:1, 0:T], 1.0 / D)
                v.tensor_mul(tt1[0:1, 0:T], row_a[0:1, :], row_a[0:1, :])
                return v.tensor_sub(row_b[0:1, :], row_b[0:1, :],
                                    tt1[0:1, 0:T])
            t_var = b.step("VE", [t_mm], f_rows)
            t_sqr = b.step("SC", [t_var], sc_act(row_b[0:1, :], row_b[0:1, :],
                                                 AF.Sqrt,
                                                 bias=eps1_c[0:1, 0:1]))

            def f_rcp(v):
                v.reciprocal(row_b[0:1, :], row_b[0:1, :])
                v.tensor_mul(row_a[0:1, :], row_a[0:1, :], row_b[0:1, :])
                return v.tensor_scalar_mul(row_a[0:1, :], row_a[0:1, :], -1.0)
            t_rows = b.step("VE", [t_sqr], f_rcp)

            def f_bc(v):
                v.matmul(ps[6][:, 0:T], ones_row[:], row_b[0:1, :],
                         start=True, stop=True)
                return v.matmul(ps[7][:, 0:T], ones_row[:], row_a[0:1, :],
                                start=True, stop=True)
            t_bc = b.step("TE", [t_rows], f_bc)

            def f_apply(v):
                last = None
                for c in range(CD):
                    sl = slice(c * T, (c + 1) * T)
                    v.tensor_mul(tt1[:, 0:T], src_t[:, sl], ps[6][:, 0:T])
                    if dst_f32 is not None:
                        last = v.tensor_add(dst_f32[:, sl], tt1[:, 0:T],
                                            ps[7][:, 0:T])
                    if dst_b16 is not None:
                        last = v.tensor_add(dst_b16[:, sl], tt1[:, 0:T],
                                            ps[7][:, 0:T])
                return last
            return b.step("VE", [t_bc], f_apply)

        def ph_rms(add_a, add_b, gain_c, dsts, deps, width):
            def rsrc(c):
                return (add_a(c), add_b(c))

            toks = list(deps)
            last_sq = None
            for c in range(CD):
                def f_r(v, c=c):
                    aa, bb_ = rsrc(c)
                    return v.tensor_add(tt1[:, 0:width], aa, bb_)
                t_r = b.step("VE", toks + [last_sq], f_r)
                t_sq = b.step("SC", [t_r], sc_act(tt2[:, 0:width],
                                                  tt1[:, 0:width], AF.Square))

                def f_mm(v, c=c):
                    return v.matmul(ps[7][0:1, 0:width], ones_col[:],
                                    tt2[:, 0:width], start=(c == 0),
                                    stop=(c == CD - 1))
                last_sq = b.step("TE", [t_sq], f_mm)
                toks = []
            t_row = b.step("VE", [last_sq], lambda v: v.tensor_scalar_mul(
                row_a[0:1, 0:width], ps[7][0:1, 0:width], 1.0 / D))
            t_sqr = b.step("SC", [t_row], sc_act(
                row_a[0:1, 0:width], row_a[0:1, 0:width], AF.Sqrt,
                bias=eps2_c[0:1, 0:1]))
            t_rcp = b.step("VE", [t_sqr], lambda v: v.reciprocal(
                row_a[0:1, 0:width], row_a[0:1, 0:width]))
            t_bc = b.step("TE", [t_rcp], lambda v: v.matmul(
                ps[7][:, 0:width], ones_row[:], row_a[0:1, 0:width],
                start=True, stop=True))

            def f_apply(v):
                last = None
                for c in range(CD):
                    aa, bb_ = rsrc(c)
                    v.tensor_add(tt1[:, 0:width], aa, bb_)
                    for dst in dsts:
                        last = v.scalar_tensor_tensor(
                            out=dst(c), in0=tt1[:, 0:width],
                            scalar=gain_c[:, c:c + 1], in1=ps[7][:, 0:width],
                            op0=ALU.mult, op1=ALU.mult)
                return last
            return b.step("VE", [t_bc], f_apply)

        def ph_ffn(x_src, dst, deps):
            with ExitStack() as fs:
                hb1_t = S_(fs, "hb1_t", (P, CH * T), BF16)
                hb2_t = S_(fs, "hb2_t", (P, CH * T), BF16)
                t_ln = ph_ln(x_src, None, nb_t, deps)
                pend = {}
                lastmul = [None]

                def evac_w12(mo, ps_ap, d):
                    m = mo // 2
                    if mo % 2 == 0:
                        pend[m] = b.step("SC", d + [lastmul[0]],
                                         sc_act(tt1[:, :], ps_ap, AF.Silu,
                                                scale=s1_c[:, m:m + 1]))
                        return pend[m]
                    t2 = b.step("SC", d + [lastmul[0]],
                                sc_act(tt2[:, :], ps_ap, AF.Copy,
                                       scale=s2_c[:, m:m + 1]))

                    def fmul(v, m=m):
                        return v.tensor_mul(hb1_t[:, m * T:(m + 1) * T],
                                            tt1[:, :], tt2[:, :])
                    lastmul[0] = b.step("VE", [t2, pend[m]], fmul)
                    return lastmul[0]

                toks = mm_phase(CD, P, 2 * CH, wAT_b,
                                lambda ki: nb_t[:, ki * T:(ki + 1) * T],
                                evac_w12, [t_ln])

                def evac_h1(mo, ps_ap, d):
                    return b.step("SC", d, sc_act(
                        hb2_t[:, mo * T:(mo + 1) * T], ps_ap, AF.Gelu,
                        scale=h1s_c[:, mo:mo + 1]))
                toks = mm_phase(CH, P, CH, h1T_b,
                                lambda ki: hb1_t[:, ki * T:(ki + 1) * T],
                                evac_h1, toks[-8:])

                def evac_h2(mo, ps_ap, d):
                    return b.step("SC", d, sc_act(
                        hb1_t[:, mo * T:(mo + 1) * T], ps_ap, AF.Gelu,
                        scale=h2s_c[:, mo:mo + 1]))
                toks = mm_phase(CH, P, CH, h2T_b,
                                lambda ki: hb2_t[:, ki * T:(ki + 1) * T],
                                evac_h2, toks[-8:])

                def evac_w3(mo, ps_ap, d):
                    def f(v, mo=mo):
                        return v.scalar_tensor_tensor(
                            out=dst[:, mo * T:(mo + 1) * T], in0=ps_ap,
                            scalar=s3h_c[:, mo:mo + 1],
                            in1=x_src[:, mo * T:(mo + 1) * T],
                            op0=ALU.mult, op1=ALU.add)
                    return b.step("VE", d, f)
                toks = mm_phase(CH, P, CD, w3T_b,
                                lambda ki: hb1_t[:, ki * T:(ki + 1) * T],
                                evac_w3, toks[-8:])
                return toks

        def ph_mixer_layer(l, u_in, u_in_b, u_out, u_out_b, deps):
            layer_toks = list(deps)
            for hf in range(2):
                with ExitStack() as ms:
                    g_t = S_(ms, "g_t", (P, CI * TH))
                    dt_t = S_(ms, "dt_t", (P, CI * TH))
                    dtu_t = S_(ms, "dtu_t", (P, CI * TH))
                    y_t = S_(ms, "y_t", (P, CI * TH))
                    Bsb_t = S_(ms, "Bsb_t", (DS, TH), BF16)
                    Csb_t = S_(ms, "Csb_t", (DS, TH), BF16)

                    with ExitStack() as m1:
                        xin_t = S_(m1, "xin_t", (P, CI * (TH + 3)))
                        xcb_t = S_(m1, "xcb_t", (P, CI * TH), BF16)
                        dtrb_t = S_(m1, "dtrb_t", (DTR, TH), BF16)
                        dtp_l = S_(m1, "dtp_l", (DTR, DI), BF16)
                        sp0 = S_(m1, "sp0", (P, TH))
                        t_dtpw = b.step("SY", layer_toks,
                                        ld(dtp_l[:], dtpT_b[l]), n_dma=1,
                                        ch="dtp")

                        def evac_in(mo, ps_ap, d):
                            if mo < CI:
                                def f(s, mo=mo):
                                    return s.copy(
                                        out=xin_t[:, mo * (TH + 3) + 3:
                                                  mo * (TH + 3) + 3 + TH],
                                        in_=ps_ap)
                                return b.step("SC", d, f)
                            m = mo - CI
                            return b.step("SC", d, sc_act(
                                g_t[:, m * TH:(m + 1) * TH], ps_ap, AF.Silu))

                        def rhs_in(ki, hf=hf):
                            return u_in_b[:, ki * T + hf * TH:
                                          ki * T + hf * TH + TH]
                        toks_in = mm_phase(CD, P, 2 * CI, inT_b[l], rhs_in,
                                           evac_in, layer_toks, N=TH)

                        conv_toks = []
                        for ci in range(CI):
                            base = ci * (TH + 3)
                            tc3 = (l * CI + ci) * 3

                            def f_conv(v, ci=ci, base=base, tc3=tc3):
                                v.tensor_copy(out=xin_t[:, base:base + 3],
                                              in_=tails[:, tc3:tc3 + 3])
                                cw = lambda k: convw_c[
                                    :, (l * 4 + k) * CI + ci:
                                    (l * 4 + k) * CI + ci + 1]
                                v.tensor_scalar_mul(
                                    sp0[:, :], xin_t[:, base:base + TH], cw(0))
                                for k in (1, 2, 3):
                                    v.scalar_tensor_tensor(
                                        out=sp0[:, :],
                                        in0=xin_t[:, base + k:base + k + TH],
                                        scalar=cw(k), in1=sp0[:, :],
                                        op0=ALU.mult, op1=ALU.add)
                                return v.tensor_copy(
                                    out=tails[:, tc3:tc3 + 3],
                                    in_=xin_t[:, base + TH:base + TH + 3])
                            prev_silu = conv_toks[-1] if conv_toks else None
                            t_c = b.step("VE", [toks_in[ci], prev_silu], f_conv)
                            t_s = b.step("SC", [t_c], sc_act(
                                xcb_t[:, ci * TH:(ci + 1) * TH], sp0[:, :],
                                AF.Silu,
                                bias=convb_c[:, l * CI + ci:l * CI + ci + 1]))
                            conv_toks.append(t_s)
                        t_xcb = conv_toks[-1]

                        def f_xp(v, l=l):
                            for ki in range(CI):
                                v.matmul(ps[0][0:DTR, 0:TH],
                                         dtr_sb[:, (l * CI + ki) * DTR:
                                                (l * CI + ki + 1) * DTR],
                                         xcb_t[:, ki * TH:(ki + 1) * TH],
                                         start=(ki == 0), stop=(ki == CI - 1))
                            for ki in range(CI):
                                v.matmul(ps[1][0:DS, 0:TH],
                                         Bw_sb[:, (l * CI + ki) * DS:
                                               (l * CI + ki + 1) * DS],
                                         xcb_t[:, ki * TH:(ki + 1) * TH],
                                         start=(ki == 0), stop=(ki == CI - 1))
                            last = None
                            for ki in range(CI):
                                last = v.matmul(
                                    ps[2][0:DS, 0:TH],
                                    Cw_sb[:, (l * CI + ki) * DS:
                                          (l * CI + ki + 1) * DS],
                                    xcb_t[:, ki * TH:(ki + 1) * TH],
                                    start=(ki == 0), stop=(ki == CI - 1))
                            return last
                        t_xp = b.step("TE", [t_xcb] + toks_in[-8:], f_xp)
                        t_dtr = b.step("SC", [t_xp], sc_act(
                            dtrb_t[:], ps[0][0:DTR, 0:TH], AF.Copy))
                        t_Bs = b.step("SC", [t_xp], sc_act(
                            Bsb_t[:], ps[1][0:DS, 0:TH], AF.Copy))
                        t_Cs = b.step("SC", [t_xp], sc_act(
                            Csb_t[:], ps[2][0:DS, 0:TH], AF.Copy))

                        dtp_toks = []
                        for g0 in range(0, CI, 8):
                            def f_dtp(v, g0=g0, l=l):
                                last = None
                                for mi in range(8):
                                    mo = g0 + mi
                                    last = v.matmul(
                                        ps[mi][:, 0:TH],
                                        dtp_l[0:DTR, mo * P:(mo + 1) * P],
                                        dtrb_t[:], start=True, stop=True)
                                return last
                            dd = [t_dtr, t_Bs, t_Cs, t_dtpw] + dtp_toks[-8:]
                            t_mm = b.step("TE", dd, f_dtp)
                            for mi in range(8):
                                mo = g0 + mi
                                bcol = dtpb_c[:, l * CI + mo:l * CI + mo + 1]

                                def f_sp(s, mi=mi, mo=mo, bcol=bcol):
                                    s.activation(out=sp0[:, :],
                                                 in_=ps[mi][:, 0:TH],
                                                 func=AF.Abs, bias=bcol)
                                    s.activation(out=sp0[:, :], in_=sp0[:, :],
                                                 func=AF.Exp, scale=-1.0)
                                    s.activation(out=sp0[:, :], in_=sp0[:, :],
                                                 func=AF.Ln, bias=1.0)
                                    return s.activation(
                                        out=dt_t[:, mo * TH:(mo + 1) * TH],
                                        in_=ps[mi][:, 0:TH],
                                        func=AF.Relu, bias=bcol)
                                t_sp = b.step("SC", [t_mm] + dtp_toks[-1:],
                                              f_sp)

                                def f_dt(v, mo=mo):
                                    sl = slice(mo * TH, (mo + 1) * TH)
                                    return v.tensor_add(
                                        dt_t[:, sl], dt_t[:, sl], sp0[:, :])
                                dtp_toks.append(b.step("VE", [t_sp], f_dt))

                        def f_dtu(v):
                            last = None
                            for ci_ in range(CI):
                                sl = slice(ci_ * TH, (ci_ + 1) * TH)
                                v.tensor_mul(dtu_t[:, sl], dt_t[:, sl],
                                             xcb_t[:, sl])
                                last = v.tensor_scalar_mul(
                                    y_t[:, sl], xcb_t[:, sl],
                                    Dp_c[:, l * CI + ci_:l * CI + ci_ + 1])
                            return last
                        t_dtu = b.step("VE", dtp_toks + conv_toks[-1:], f_dtu)

                    with ExitStack() as m3:
                        Bb_t = S_(m3, "Bb_t", (P, DS * TH), BF16)
                        Cb_t = S_(m3, "Cb_t", (P, DS * TH), BF16)
                        ab_t = S_(m3, "ab_t", (P, NAB * TH))
                        bb_t = S_(m3, "bb_t", (P, NAB * TH))
                        yp_t = S_(m3, "yp_t", (P, NYP * TH))

                        bc_toks = []
                        for s_ in range(DS):
                            def f_mmb(v, s_=s_):
                                v.matmul(ps[4][:, 0:TH],
                                         selp_b[:, s_ * P:(s_ + 1) * P],
                                         Bsb_t[:], start=True, stop=True)
                                return v.matmul(ps[5][:, 0:TH],
                                                selp_b[:, s_ * P:(s_ + 1) * P],
                                                Csb_t[:], start=True, stop=True)
                            dd = [t_Bs, t_Cs, t_dtu] + (
                                bc_toks[-8:] if bc_toks else dtp_toks[-8:])
                            t_bm = b.step("TE", dd, f_mmb)
                            t_eb = b.step("SC", [t_bm], sc_act(
                                Bb_t[:, s_ * TH:(s_ + 1) * TH],
                                ps[4][:, 0:TH], AF.Copy))
                            t_ec = b.step("SC", [t_bm], sc_act(
                                Cb_t[:, s_ * TH:(s_ + 1) * TH],
                                ps[5][:, 0:TH], AF.Copy))
                            bc_toks += [t_eb, t_ec]

                        t_red_prev = None
                        y_toks = []
                        ab_toks = [None] * NAB
                        bb_toks = [None] * NAB
                        yp_toks = [None] * NYP
                        for ci_ in range(CI):
                            acol0 = (l * CI + ci_) * DS
                            a_done = []
                            b_done = []
                            for q0 in range(0, DS, NAB):
                                def f_a(s, ci_=ci_, q0=q0, acol0=acol0):
                                    last = None
                                    for qi in range(NAB):
                                        s_ = q0 + qi
                                        last = s.activation(
                                            out=ab_t[:, qi * TH:(qi + 1) * TH],
                                            in_=dt_t[:, ci_ * TH:
                                                     (ci_ + 1) * TH],
                                            func=AF.Exp,
                                            scale=A_cols[:, acol0 + s_:
                                                         acol0 + s_ + 1])
                                    return last
                                war_a = [ab_toks[q] for q in range(NAB)]
                                t_a = b.step("SC", [t_dtu] + war_a +
                                             bc_toks[-1:], f_a)

                                def f_b(g, ci_=ci_, q0=q0):
                                    last = None
                                    for qi in range(NAB):
                                        s_ = q0 + qi
                                        last = g.tensor_mul(
                                            bb_t[:, qi * TH:(qi + 1) * TH],
                                            dtu_t[:, ci_ * TH:(ci_ + 1) * TH],
                                            Bb_t[:, s_ * TH:(s_ + 1) * TH])
                                    return last
                                war_b = [bb_toks[q] for q in range(NAB)]
                                t_b = b.step("GP", [t_dtu] + war_b +
                                             bc_toks[-2:-1], f_b)

                                for qi in range(NAB):
                                    s_ = q0 + qi
                                    hcol = acol0 + s_
                                    yslot = s_ % NYP

                                    def f_scan(v, qi=qi, s_=s_, hcol=hcol,
                                               yslot=yslot):
                                        sl = slice(yslot * TH,
                                                   (yslot + 1) * TH)
                                        v.tensor_tensor_scan(
                                            out=yp_t[:, sl],
                                            data0=ab_t[:, qi * TH:
                                                       (qi + 1) * TH],
                                            data1=bb_t[:, qi * TH:
                                                       (qi + 1) * TH],
                                            initial=H_state[:, hcol:hcol + 1],
                                            op0=ALU.mult, op1=ALU.add)
                                        return v.tensor_copy(
                                            out=H_state[:, hcol:hcol + 1],
                                            in_=yp_t[:, yslot * TH + TH - 1:
                                                     yslot * TH + TH])
                                    t_sc = b.step(
                                        "VE", [t_a, t_b, yp_toks[yslot]],
                                        f_scan)
                                    ab_toks[qi] = t_sc
                                    bb_toks[qi] = t_sc
                                    if s_ < NYP:
                                        a_done.append(t_sc)
                                    else:
                                        b_done.append(t_sc)
                                    yp_toks[yslot] = t_sc

                                if q0 + NAB == NYP:
                                    def f_yc1(g, ci_=ci_):
                                        last = None
                                        for s_ in range(NYP):
                                            sl = slice(s_ * TH, (s_ + 1) * TH)
                                            last = g.tensor_mul(
                                                yp_t[:, sl], yp_t[:, sl],
                                                Cb_t[:, sl])
                                        return last
                                    t_gp1 = b.step("GP", a_done, f_yc1)

                                    def f_red1(v):
                                        return v.tensor_reduce(
                                            out=tt1[:, 0:TH],
                                            in_=yp_t[:].rearrange(
                                                "p (s t) -> p t s", s=NYP),
                                            axis=mybir.AxisListType.X,
                                            op=ALU.add)
                                    t_red1 = b.step("VE", [t_gp1], f_red1)
                                    for q in range(NYP):
                                        yp_toks[q] = t_red1

                            def f_yc2(g, ci_=ci_):
                                last = None
                                for s_ in range(NYP, DS):
                                    sl = slice((s_ - NYP) * TH,
                                               (s_ - NYP + 1) * TH)
                                    last = g.tensor_mul(
                                        yp_t[:, sl], yp_t[:, sl],
                                        Cb_t[:, s_ * TH:(s_ + 1) * TH])
                                return last
                            t_gp2 = b.step("GP", b_done, f_yc2)

                            def f_red2(v, ci_=ci_):
                                v.tensor_reduce(
                                    out=tt2[:, 0:TH],
                                    in_=yp_t[:].rearrange(
                                        "p (s t) -> p t s", s=NYP),
                                    axis=mybir.AxisListType.X, op=ALU.add)
                                sl = slice(ci_ * TH, (ci_ + 1) * TH)
                                v.tensor_add(tt1[:, 0:TH], tt1[:, 0:TH],
                                             tt2[:, 0:TH])
                                return v.tensor_add(y_t[:, sl], y_t[:, sl],
                                                    tt1[:, 0:TH])
                            t_red_prev = b.step("VE", [t_gp2], f_red2)
                            for q in range(NYP):
                                yp_toks[q] = t_red_prev
                            y_toks.append(t_red_prev)

                    with ExitStack() as m4:
                        y2b_t = S_(m4, "y2b_t", (P, CI * TH), BF16)
                        vb_t = S_(m4, "vb_t", (P, CD * TH), BF16)
                        hmb_t = S_(m4, "hmb_t", (P, (DI // P) * TH), BF16)
                        m2_t = S_(m4, "m2_t", (P, CD * TH))

                        def f_gate(g):
                            last = None
                            for ci_ in range(CI):
                                sl = slice(ci_ * TH, (ci_ + 1) * TH)
                                last = g.tensor_mul(y2b_t[:, sl], y_t[:, sl],
                                                    g_t[:, sl])
                            return last
                        t_gate = b.step("GP", y_toks, f_gate)

                        def evac_out(mo, ps_ap, d):
                            return b.step("SC", d, sc_act(
                                vb_t[:, mo * TH:(mo + 1) * TH], ps_ap,
                                AF.Copy))
                        toks_o = mm_phase(
                            CI, P, CD, outT_b[l],
                            lambda ki: y2b_t[:, ki * TH:(ki + 1) * TH],
                            evac_out, [t_gate], N=TH)

                        def evac_m1(mo, ps_ap, d):
                            return b.step("SC", d, sc_act(
                                hmb_t[:, mo * TH:(mo + 1) * TH], ps_ap,
                                AF.Gelu, bias=mlpb1_c[:, mo:mo + 1]))
                        toks_m1 = mm_phase(
                            CD, P, DI // P, mlp1T_b,
                            lambda ki: vb_t[:, ki * TH:(ki + 1) * TH],
                            evac_m1, toks_o[-8:], N=TH)

                        def evac_m2(mo, ps_ap, d):
                            return b.step("SC", d, sc_act(
                                m2_t[:, mo * TH:(mo + 1) * TH], ps_ap,
                                AF.Gelu, bias=mlpb2_c[:, mo:mo + 1]))
                        toks_m2 = mm_phase(
                            DI // P, P, CD, mlp2T_b,
                            lambda ki: hmb_t[:, ki * TH:(ki + 1) * TH],
                            evac_m2, toks_m1[-8:], N=TH)

                        t_rms = ph_rms(
                            lambda c: m2_t[:, c * TH:(c + 1) * TH],
                            lambda c, hf=hf: u_in[:, c * T + hf * TH:
                                                  c * T + hf * TH + TH],
                            lnw_c,
                            [lambda c, hf=hf: u_out[:, c * T + hf * TH:
                                                    c * T + hf * TH + TH],
                             lambda c, hf=hf: u_out_b[:, c * T + hf * TH:
                                                      c * T + hf * TH + TH]],
                            toks_m2, TH)
                        layer_toks = [t_rms]
            return layer_toks

        toks = list(start_toks)
        dbg_requests = set(debug_taps)
        dbg_tensors = {}

        def dbg(name, j, ap, deps):
            if (name, j) in dbg_requests:
                o = nc.dram_tensor(f"dbg_{name}_{j}", list(ap.shape), F32,
                                   kind="ExternalOutput").ap()
                return [b.step("SY", deps, ld(o, ap), n_dma=1, ch="dbg")]
            return []

        t_store_last = None
        for j in range(n_tiles):
            b.seg = f"t{j}"
            t_x = b.step("SY", toks, ld(
                x_t[:].rearrange("p (c t) -> p c t", t=T),
                x_d.rearrange("p (c t) -> p c t", t=L)[:, :, j * T:(j + 1) * T]
                ), n_dma=1, ch="xl")
            toks = ph_ffn(x_t, x1_t, toks + [t_x])
            toks += dbg("x1", j, x1_t[:], toks)
            t_u0 = ph_ln(x1_t, u0_t, None, toks)
            t_u0b = b.step("GP", [t_u0], lambda g: g.tensor_copy(
                out=nb_t[:], in_=u0_t[:]))
            lt = ph_mixer_layer(0, u0_t, nb_t, ua_t, nb2_t, [t_u0, t_u0b])
            toks = lt + dbg("u1", j, ua_t[:], lt)
            lt = ph_mixer_layer(1, ua_t, nb2_t, x_t, nb_t, toks)
            toks = lt + dbg("u2", j, x_t[:], lt)
            t_mo = ph_rms(lambda c: x_t[:, c * T:(c + 1) * T],
                          lambda c: u0_t[:, c * T:(c + 1) * T],
                          ln2w_c,
                          [lambda c: u0_t[:, c * T:(c + 1) * T]],
                          toks, T)

            def f_x2(v):
                last = None
                for c in range(CD):
                    sl = slice(c * T, (c + 1) * T)
                    last = v.tensor_add(x1_t[:, sl], x1_t[:, sl], u0_t[:, sl])
                return last
            t_x2 = b.step("VE", [t_mo], f_x2)
            toks = [t_x2] + dbg("x2", j, x1_t[:], [t_x2])
            toks = ph_ffn(x1_t, x_t, toks)
            t_store_last = b.step("SY", toks, ld(
                out_d.rearrange("p (c t) -> p c t", t=L)[:, :, j * T:(j + 1) * T],
                x_t[:].rearrange("p (c t) -> p c t", t=T)), n_dma=1, ch="st")
            toks = toks + [t_store_last]

        b.step("GP", [t_store_last], lambda g: g.memset(tt1[0:1, 0:1], 0.0),
               inc=False)

        _emit(nc, b, es)
    return nc


def _selp_const():
    sp = np.zeros((DS, DS * P), np.float32)
    for s in range(DS):
        sp[s, s * P:(s + 1) * P] = 1.0
    return sp


def _relay(inputs):
    f = np.ascontiguousarray
    xp = inputs["L0_attn_x_proj_w"]
    w1T = inputs["L0_ffn_w1_w"].T
    w2T = inputs["L0_ffn_w2_w"].T
    wAT = np.stack([w1T.reshape(D, CH, P), w2T.reshape(D, CH, P)],
                   axis=2).reshape(D, 2 * DH)
    common = {
        "wAT": f(wAT),
        "h1T": f(inputs["L0_ffn_h1_w"].T),
        "h2T": f(inputs["L0_ffn_h2_w"].T),
        "w3T": f(inputs["L0_ffn_w3_w"].T),
        "inT": f(inputs["L0_attn_in_proj_w"].transpose(0, 2, 1)),
        "outT": f(inputs["L0_attn_out_proj_w"].transpose(0, 2, 1)),
        "mlp1T": f(inputs["L0_attn_mlp_w1"].T),
        "mlp2T": f(inputs["L0_attn_mlp_w2"].T),
        "dtpT": f(inputs["L0_attn_dt_proj_w"].transpose(0, 2, 1)),
        "dtrT": f(xp[:, :DTR, :].transpose(0, 2, 1)),
        "BT": f(xp[:, DTR:DTR + DS, :].transpose(0, 2, 1)),
        "CT": f(xp[:, DTR + DS:, :].transpose(0, 2, 1)),
        "alog": f(inputs["L0_attn_A_log"].reshape(2, CI, P, DS)
                  .transpose(2, 0, 1, 3).reshape(P, 2 * CI * DS)),
        "convw": f(inputs["L0_attn_conv_w"].reshape(2, CI, P, 4)
                   .transpose(2, 0, 3, 1).reshape(P, 2 * 4 * CI)),
        "convb": f(inputs["L0_attn_conv_b"].reshape(2, CI, P)
                   .transpose(2, 0, 1).reshape(P, 2 * CI)),
        "dtpb": f(inputs["L0_attn_dt_proj_b"].reshape(2, CI, P)
                  .transpose(2, 0, 1).reshape(P, 2 * CI)),
        "Dp": f(inputs["L0_attn_D_param"].reshape(2, CI, P)
                .transpose(2, 0, 1).reshape(P, 2 * CI)),
        "mlpb1": f(inputs["L0_attn_mlp_b1"].reshape(DI // P, P).T),
        "mlpb2": f(inputs["L0_attn_mlp_b2"].reshape(CD, P).T),
        "lnw": f(inputs["L0_attn_ln_w"].reshape(CD, P).T),
        "ln2w": f(inputs["L0_attn_ln2_w"].reshape(CD, P).T),
        "s1": f(inputs["L0_ffn_w1_s"].reshape(CH, P).T),
        "s2": f(inputs["L0_ffn_w2_s"].reshape(CH, P).T),
        "h1s": f(inputs["L0_ffn_h1_s"].reshape(CH, P).T),
        "h2s": f(inputs["L0_ffn_h2_s"].reshape(CH, P).T),
        "s3": f(inputs["L0_ffn_w3_s"].reshape(CD, P).T),
        "selp": _selp_const(),
    }
    return {k: np.asarray(v, np.float32) for k, v in common.items()}


def kernel(**inputs):
    inputs = {k: np.asarray(v) for k, v in inputs.items()}
    common = _relay(inputs)
    x = inputs["L0_attn_x"]
    n_cores = 8
    in_maps = []
    for c in range(n_cores):
        m = dict(common)
        xc_ = x[c].T.reshape(CD, P, L).transpose(1, 0, 2).reshape(P, CD * L)
        m["x"] = np.ascontiguousarray(xc_).astype(np.float32)
        in_maps.append(m)
    nc = build_program()
    res = run_bass_kernel_spmd(nc, in_maps, core_ids=list(range(n_cores)))
    outs = []
    for c in range(n_cores):
        o = res.results[c]["out"].reshape(P, CD, L).transpose(1, 0, 2)
        outs.append(o.reshape(D, L).T)
    return np.ascontiguousarray(np.stack(outs, 0)).astype(np.float32)


if __name__ == "__main__":
    import reference as R
    inp = R.setup_inputs(0)
    out = kernel(**{k: np.asarray(v) for k, v in inp.items()})
    print(out.shape, out.dtype)


# version 04
# speedup vs baseline: 1.0928x; 1.0928x over previous
import numpy as np
from contextlib import ExitStack

import concourse.bass as bass
import concourse.mybir as mybir
from concourse.bass_utils import run_bass_kernel_spmd

F32 = mybir.dt.float32
BF16 = mybir.dt.bfloat16
AF = mybir.ActivationFunctionType
ALU = mybir.AluOpType

P = 128
D = 1024
DI = 2048
DH = 4096
DS = 16
DTR = 64
L = 4096
T = 512
TH = 256
NT = L // T
CD = D // P
CI = DI // P
CH = DH // P
LN_EPS = 1e-5
RMS_EPS = 1.1920929e-07
NAB = 2
NYP = 8

ENGS = ("TE", "VE", "SC", "GP", "SY")


class Bld:

    def __init__(self):
        self.steps = []
        self.counts = {}
        self.seg = "s"

    def step(self, eng, deps, fn, inc=True, n_dma=0, ch=None):
        if n_dma:
            key = ("DMA", ch if ch is not None else f"{eng}_{self.seg}")
        else:
            key = (eng, self.seg)
        deps = [d for d in deps if d is not None]
        self.steps.append((eng, key, deps, fn, inc, n_dma))
        if key not in self.counts:
            self.counts[key] = 0
        if n_dma:
            self.counts[key] += 16 * n_dma
        elif inc:
            self.counts[key] += 1
        return (key, self.counts[key])


def _emit(nc, b, es):
    sems = {k: es.enter_context(nc.semaphore(f"S_{k[0]}_{k[1]}"))
            for k in b.counts}
    blk = es.enter_context(nc.Block())

    def body_for(ename):
        def body(eng):
            last_wait = {}
            for (e, key, deps, fn, inc, n_dma) in b.steps:
                if e != ename:
                    continue
                for (dkey, dval) in deps:
                    if dkey[0] == ename:
                        continue
                    if dval > last_wait.get(dkey, 0):
                        eng.wait_ge(sems[dkey], dval)
                        last_wait[dkey] = dval
                if n_dma:
                    fn(eng, sems[key])
                else:
                    inst = fn(eng)
                    if inc:
                        inst.then_inc(sems[key], 1)
        return body

    blk.tensor(body_for("TE"))
    blk.vector(body_for("VE"))
    blk.scalar(body_for("SC"))
    blk.gpsimd(body_for("GP"))
    blk.sync(body_for("SY"))


def ceil_div(a, bb):
    return (a + bb - 1) // bb


def build_program(n_tiles=NT, debug_taps=()):
    nc = bass.Bass("TRN2", target_bir_lowering=False,
                   detect_race_conditions=False)

    def din(name, shape, dt=F32):
        return nc.dram_tensor(name, list(shape), dt, kind="ExternalInput").ap()

    x_d = din("x", (P, CD * L))
    out_d = nc.dram_tensor("out", [P, CD * L], F32, kind="ExternalOutput").ap()

    wAT = din("wAT", (D, 2 * DH))
    h1T = din("h1T", (DH, DH))
    h2T = din("h2T", (DH, DH))
    w3T = din("w3T", (DH, D))
    inT = din("inT", (2, D, 2 * DI))
    outT = din("outT", (2, DI, D))
    mlp1T = din("mlp1T", (D, 2 * D))
    mlp2T = din("mlp2T", (2 * D, D))
    dtpT = din("dtpT", (2, DTR, DI))
    dtrT = din("dtrT", (2, DI, DTR))
    BT = din("BT", (2, DI, DS))
    CT = din("CT", (2, DI, DS))
    alog_d = din("alog", (P, 2 * CI * DS))
    convw_d = din("convw", (P, 2 * 4 * CI))
    convb_d = din("convb", (P, 2 * CI))
    dtpb_d = din("dtpb", (P, 2 * CI))
    Dp_d = din("Dp", (P, 2 * CI))
    mlpb1_d = din("mlpb1", (P, DI // P))
    mlpb2_d = din("mlpb2", (P, CD))
    lnw_d = din("lnw", (P, CD))
    ln2w_d = din("ln2w", (P, CD))
    s1_d = din("s1", (P, CH))
    s2_d = din("s2", (P, CH))
    h1s_d = din("h1s", (P, CH))
    h2s_d = din("h2s", (P, CH))
    s3_d = din("s3", (P, CD))
    selp_d = din("selp", (DS, DS * P))

    def dint(name, shape):
        return nc.dram_tensor(name, list(shape), BF16, kind="Internal").ap()

    wAT_b = dint("wAT_b", (D, 2 * DH))
    h1T_b = dint("h1T_b", (DH, DH))
    h2T_b = dint("h2T_b", (DH, DH))
    w3T_b = dint("w3T_b", (DH, D))
    inT_b = dint("inT_b", (2, D, 2 * DI))
    outT_b = dint("outT_b", (2, DI, D))
    mlp1T_b = dint("mlp1T_b", (D, 2 * D))
    mlp2T_b = dint("mlp2T_b", (2 * D, D))
    dtpT_b = dint("dtpT_b", (2, DTR, DI))
    dtrT_b = dint("dtrT_b", (2, DI, DTR))
    BT_b = dint("BT_b", (2, DI, DS))
    CT_b = dint("CT_b", (2, DI, DS))

    b = Bld()
    es = ExitStack()
    _uid = [0]
    with es:
        with nc.sbuf_tensor("wipe", [128, 53000], F32) as wipe_t:
            t_wipe = b.step("VE", [],
                            (lambda w: lambda v: v.memset(w[:], 0.0))(wipe_t))
        def S_(stk, name, shape, dt=F32):
            _uid[0] += 1
            return stk.enter_context(
                nc.sbuf_tensor(f"{name}_{_uid[0]}", list(shape), dt))

        def S(name, shape, dt=F32):
            return S_(es, name, shape, dt)

        A_cols = S("A_cols", (P, 2 * CI * DS))
        convw_c = S("convw_c", (P, 2 * 4 * CI))
        convb_c = S("convb_c", (P, 2 * CI))
        dtpb_c = S("dtpb_c", (P, 2 * CI))
        Dp_c = S("Dp_c", (P, 2 * CI))
        mlpb1_c = S("mlpb1_c", (P, DI // P))
        mlpb2_c = S("mlpb2_c", (P, CD))
        lnw_c = S("lnw_c", (P, CD))
        ln2w_c = S("ln2w_c", (P, CD))
        s1_c = S("s1_c", (P, CH))
        s2_c = S("s2_c", (P, CH))
        h1s_c = S("h1s_c", (P, CH))
        h2s_c = S("h2s_c", (P, CH))
        s3h_c = S("s3h_c", (P, CD))
        selp_b = S("selp_b", (DS, DS * P), BF16)
        ones_col = S("ones_col", (P, 1))
        ones_row = S("ones_row", (1, P))
        H_state = S("H_state", (P, 2 * CI * DS))
        tails = S("tails", (P, 2 * CI * 3))
        dtr_sb = S("dtr_sb", (P, 2 * CI * DTR), BF16)
        Bw_sb = S("Bw_sb", (P, 2 * CI * DS), BF16)
        Cw_sb = S("Cw_sb", (P, 2 * CI * DS), BF16)
        row_a = S("row_a", (1, T))
        row_b = S("row_b", (1, T))
        eps1_c = S("eps1_c", (P, 1))
        eps2_c = S("eps2_c", (P, 1))
        tt1 = S("tt1", (P, T))
        tt2 = S("tt2", (P, T))
        NWB = 3
        wbuf = [S(f"wbuf{i}", (P, 8 * P), BF16) for i in range(NWB)]

        x_t = S("x_t", (P, CD * T))
        x1_t = S("x1_t", (P, CD * T))
        u0_t = S("u0_t", (P, CD * T))
        ua_t = S("ua_t", (P, CD * T))
        nb_t = S("nb_t", (P, CD * T), BF16)
        nb2_t = S("nb2_t", (P, CD * T), BF16)

        ps = [es.enter_context(nc.psum_tensor(f"ps{i}", [P, T], F32))
              for i in range(8)]

        def ld(dst_ap, src_ap):
            def f(eng, sem):
                eng.dma_start(out=dst_ap, in_=src_ap).then_inc(sem, 16)
            return f

        wb_idx = [0]
        wb_tok = [None] * 4
        t_ld = None
        for dst, src in [
            (convw_c[:], convw_d), (convb_c[:], convb_d),
            (dtpb_c[:], dtpb_d), (Dp_c[:], Dp_d),
            (mlpb1_c[:], mlpb1_d), (mlpb2_c[:], mlpb2_d),
            (lnw_c[:], lnw_d), (ln2w_c[:], ln2w_d),
            (s1_c[:], s1_d), (s2_c[:], s2_d),
            (h1s_c[:], h1s_d), (h2s_c[:], h2s_d),
            (A_cols[:], alog_d), (s3h_c[:], s3_d),
        ]:
            t_ld = b.step("SY", [t_wipe], ld(dst, src), n_dma=1, ch="const")
        t_selp_raw = b.step("SY", [], ld(tt1[0:DS, 0:T], selp_d[:, 0:T]),
                            n_dma=1, ch="const")
        t_selp_raw = b.step("SY", [], ld(tt2[0:DS, 0:T], selp_d[:, T:2 * T]),
                            n_dma=1, ch="const")
        t_const_all = t_selp_raw

        def f_selp(v):
            v.tensor_copy(out=selp_b[:, 0:T], in_=tt1[0:DS, 0:T])
            return v.tensor_copy(out=selp_b[:, T:2 * T], in_=tt2[0:DS, 0:T])
        t_selp = b.step("VE", [t_selp_raw], f_selp)
        t_selp_raw2 = b.step("SY", [t_selp], ld(tt1[0:DS, 0:T],
                                                selp_d[:, 2 * T:3 * T]),
                             n_dma=1, ch="const2")
        t_selp_raw2 = b.step("SY", [], ld(tt2[0:DS, 0:T],
                                          selp_d[:, 3 * T:4 * T]),
                             n_dma=1, ch="const2")

        def f_selp2(v):
            v.tensor_copy(out=selp_b[:, 2 * T:3 * T], in_=tt1[0:DS, 0:T])
            return v.tensor_copy(out=selp_b[:, 3 * T:4 * T], in_=tt2[0:DS, 0:T])
        t_selp = b.step("VE", [t_selp_raw2], f_selp2)
        b.step("VE", [], lambda v: v.memset(ones_col[:], 1.0), inc=False)
        b.step("VE", [], lambda v: v.memset(eps1_c[:], LN_EPS), inc=False)
        t_eps = b.step("VE", [], lambda v: v.memset(eps2_c[:], RMS_EPS))
        b.step("VE", [], lambda v: v.memset(ones_row[:], 1.0), inc=False)
        b.step("VE", [], lambda v: v.memset(H_state[:], 0.0), inc=False)
        b.step("VE", [], lambda v: v.memset(tails[:], 0.0), inc=False)
        t_a1 = b.step("SC", [t_const_all], lambda s: s.activation(
            out=A_cols[:], in_=A_cols[:], func=AF.Exp))
        t_a2 = b.step("VE", [t_a1], lambda v: v.tensor_scalar_mul(
            A_cols[:], A_cols[:], -1.0))
        t_s3 = b.step("VE", [t_const_all], lambda v: v.tensor_scalar_mul(
            s3h_c[:], s3h_c[:], 0.5))

        CW = 2048
        NCV = 4
        with ExitStack() as cvs:
            cvt = [S_(cvs, f"cvt{i}", (P, CW)) for i in range(NCV)]
            cvtb = [S_(cvs, f"cvtb{i}", (P, CW), BF16) for i in range(NCV)]
            cvt_idx = [0]
            cvt_tok = [None] * NCV

            def convert(df, db, R, C):
                for i in range(ceil_div(R, P)):
                    r0 = i * P
                    pp = min(P, R - r0)
                    for c0 in range(0, C, CW):
                        cc = min(CW, C - c0)
                        bi = cvt_idx[0] % NCV
                        cvt_idx[0] += 1
                        t_in = b.step("SY", [cvt_tok[bi]],
                                      ld(cvt[bi][0:pp, 0:cc],
                                         df[r0:r0 + pp, c0:c0 + cc]),
                                      n_dma=1, ch=f"cvi{bi}")

                        def fcast(v, bi=bi, pp=pp, cc=cc):
                            return v.tensor_copy(out=cvtb[bi][0:pp, 0:cc],
                                                 in_=cvt[bi][0:pp, 0:cc])
                        t_cast = b.step("VE", [t_in], fcast)
                        t_out = b.step("SY", [t_cast],
                                       ld(db[r0:r0 + pp, c0:c0 + cc],
                                          cvtb[bi][0:pp, 0:cc]),
                                       n_dma=1, ch=f"cvo{bi}")
                        cvt_tok[bi] = t_out

            convert(wAT, wAT_b, D, 2 * DH)
            convert(h1T, h1T_b, DH, DH)
            convert(h2T, h2T_b, DH, DH)
            convert(w3T, w3T_b, DH, D)
            convert(mlp1T, mlp1T_b, D, 2 * D)
            convert(mlp2T, mlp2T_b, 2 * D, D)
            for l in range(2):
                convert(inT[l], inT_b[l], D, 2 * DI)
                convert(outT[l], outT_b[l], DI, D)
                convert(dtpT[l], dtpT_b[l], DTR, DI)
                convert(dtrT[l], dtrT_b[l], DI, DTR)
                convert(BT[l], BT_b[l], DI, DS)
                convert(CT[l], CT_b[l], DI, DS)
            cvt_done = [t for t in cvt_tok if t is not None]

        def f_small(eng, sem):
            for l in range(2):
                for ki in range(CI):
                    eng.dma_start(
                        out=dtr_sb[:, (l * CI + ki) * DTR:
                                   (l * CI + ki + 1) * DTR],
                        in_=dtrT_b[l][ki * P:(ki + 1) * P, :]).then_inc(sem, 16)
                    eng.dma_start(
                        out=Bw_sb[:, (l * CI + ki) * DS:(l * CI + ki + 1) * DS],
                        in_=BT_b[l][ki * P:(ki + 1) * P, :]).then_inc(sem, 16)
                    eng.dma_start(
                        out=Cw_sb[:, (l * CI + ki) * DS:(l * CI + ki + 1) * DS],
                        in_=CT_b[l][ki * P:(ki + 1) * P, :]).then_inc(sem, 16)
        t_small = b.step("SY", cvt_done, f_small, n_dma=2 * 3 * CI,
                         ch="small")

        start_toks = cvt_done + [t_small, t_a2, t_s3, t_selp, t_eps, t_const_all]
        for _i in range(NWB):
            wb_tok[_i] = t_small


        def sc_act(dst, src, func, scale=1.0, bias=0.0):
            def f(s):
                return s.activation(out=dst, in_=src, func=func,
                                    scale=scale, bias=bias)
            return f

        def mm_phase(Kch, kp, Mch, wdram, rhs_fn, evac_fn, deps, N=T):
            group = 8
            out_toks = []
            prev_evacs = []
            for g0 in range(0, Mch, group):
                gm = min(group, Mch - g0)
                t_te = None
                for ki in range(Kch):
                    bi = wb_idx[0] % NWB
                    wb_idx[0] += 1
                    t_w = b.step("SY", [wb_tok[bi]],
                                 ld(wbuf[bi][0:kp, 0:gm * P],
                                    wdram[ki * kp:ki * kp + kp,
                                          g0 * P:(g0 + gm) * P]),
                                 n_dma=1, ch=f"wq{bi}")

                    def fmm(v, bi=bi, ki=ki, gm=gm):
                        last = None
                        rhs = rhs_fn(ki)
                        for mi in range(gm):
                            last = v.matmul(ps[mi][:, 0:N],
                                            wbuf[bi][0:kp, mi * P:(mi + 1) * P],
                                            rhs, start=(ki == 0),
                                            stop=(ki == Kch - 1))
                        return last
                    dd = list(deps) + [t_w]
                    if ki == 0:
                        dd += prev_evacs
                    t_te = b.step("TE", dd, fmm)
                    wb_tok[bi] = t_te
                prev_evacs = []
                for mi in range(gm):
                    tok = evac_fn(g0 + mi, ps[mi][:, 0:N], [t_te])
                    prev_evacs.append(tok)
                    out_toks.append(tok)
            return out_toks

        def ph_norm_stats(src_fn, nch, width, deps, want_mean):
            toks = list(deps)
            last_sq = None
            for c in range(nch):
                def f_cp(v, c=c):
                    return v.tensor_copy(out=tt1[:, 0:width], in_=src_fn(c))
                t_cp = b.step("VE", toks + [last_sq], f_cp)
                t_sq = b.step("SC", [t_cp], sc_act(tt2[:, 0:width],
                                                   tt1[:, 0:width], AF.Square))

                def f_mm(v, c=c):
                    if want_mean:
                        v.matmul(ps[6][0:1, 0:width], ones_col[:],
                                 tt1[:, 0:width], start=(c == 0),
                                 stop=(c == nch - 1))
                    return v.matmul(ps[7][0:1, 0:width], ones_col[:],
                                    tt2[:, 0:width], start=(c == 0),
                                    stop=(c == nch - 1))
                last_sq = b.step("TE", [t_sq], f_mm)
                toks = []
            return last_sq

        def ph_ln(src_t, dst_f32, dst_b16, deps):
            t_mm = ph_norm_stats(lambda c: src_t[:, c * T:(c + 1) * T],
                                 CD, T, deps, True)

            def f_rows(v):
                v.tensor_scalar_mul(row_a[0:1, :], ps[6][0:1, 0:T], 1.0 / D)
                v.tensor_scalar_mul(row_b[0:1, :], ps[7][0:1, 0:T], 1.0 / D)
                v.tensor_mul(tt1[0:1, 0:T], row_a[0:1, :], row_a[0:1, :])
                return v.tensor_sub(row_b[0:1, :], row_b[0:1, :],
                                    tt1[0:1, 0:T])
            t_var = b.step("VE", [t_mm], f_rows)
            t_sqr = b.step("SC", [t_var], sc_act(row_b[0:1, :], row_b[0:1, :],
                                                 AF.Sqrt,
                                                 bias=eps1_c[0:1, 0:1]))

            def f_rcp(v):
                v.reciprocal(row_b[0:1, :], row_b[0:1, :])
                v.tensor_mul(row_a[0:1, :], row_a[0:1, :], row_b[0:1, :])
                return v.tensor_scalar_mul(row_a[0:1, :], row_a[0:1, :], -1.0)
            t_rows = b.step("VE", [t_sqr], f_rcp)

            def f_bc(v):
                v.matmul(ps[6][:, 0:T], ones_row[:], row_b[0:1, :],
                         start=True, stop=True)
                return v.matmul(ps[7][:, 0:T], ones_row[:], row_a[0:1, :],
                                start=True, stop=True)
            t_bc = b.step("TE", [t_rows], f_bc)

            def f_apply(v):
                last = None
                for c in range(CD):
                    sl = slice(c * T, (c + 1) * T)
                    v.tensor_mul(tt1[:, 0:T], src_t[:, sl], ps[6][:, 0:T])
                    if dst_f32 is not None:
                        last = v.tensor_add(dst_f32[:, sl], tt1[:, 0:T],
                                            ps[7][:, 0:T])
                    if dst_b16 is not None:
                        last = v.tensor_add(dst_b16[:, sl], tt1[:, 0:T],
                                            ps[7][:, 0:T])
                return last
            return b.step("VE", [t_bc], f_apply)

        def ph_rms(add_a, add_b, gain_c, dsts, deps, width):
            def rsrc(c):
                return (add_a(c), add_b(c))

            toks = list(deps)
            last_sq = None
            for c in range(CD):
                def f_r(v, c=c):
                    aa, bb_ = rsrc(c)
                    return v.tensor_add(tt1[:, 0:width], aa, bb_)
                t_r = b.step("VE", toks + [last_sq], f_r)
                t_sq = b.step("SC", [t_r], sc_act(tt2[:, 0:width],
                                                  tt1[:, 0:width], AF.Square))

                def f_mm(v, c=c):
                    return v.matmul(ps[7][0:1, 0:width], ones_col[:],
                                    tt2[:, 0:width], start=(c == 0),
                                    stop=(c == CD - 1))
                last_sq = b.step("TE", [t_sq], f_mm)
                toks = []
            t_row = b.step("VE", [last_sq], lambda v: v.tensor_scalar_mul(
                row_a[0:1, 0:width], ps[7][0:1, 0:width], 1.0 / D))
            t_sqr = b.step("SC", [t_row], sc_act(
                row_a[0:1, 0:width], row_a[0:1, 0:width], AF.Sqrt,
                bias=eps2_c[0:1, 0:1]))
            t_rcp = b.step("VE", [t_sqr], lambda v: v.reciprocal(
                row_a[0:1, 0:width], row_a[0:1, 0:width]))
            t_bc = b.step("TE", [t_rcp], lambda v: v.matmul(
                ps[7][:, 0:width], ones_row[:], row_a[0:1, 0:width],
                start=True, stop=True))

            def f_apply(v):
                last = None
                for c in range(CD):
                    aa, bb_ = rsrc(c)
                    v.tensor_add(tt1[:, 0:width], aa, bb_)
                    for dst in dsts:
                        last = v.scalar_tensor_tensor(
                            out=dst(c), in0=tt1[:, 0:width],
                            scalar=gain_c[:, c:c + 1], in1=ps[7][:, 0:width],
                            op0=ALU.mult, op1=ALU.mult)
                return last
            return b.step("VE", [t_bc], f_apply)

        def ph_ffn(x_src, dst, deps):
            with ExitStack() as fs:
                hb1_t = S_(fs, "hb1_t", (P, CH * T), BF16)
                hb2_t = S_(fs, "hb2_t", (P, CH * T), BF16)
                t_ln = ph_ln(x_src, None, nb_t, deps)
                pend = {}
                lastmul = [None]

                def evac_w12(mo, ps_ap, d):
                    m = mo // 2
                    if mo % 2 == 0:
                        pend[m] = b.step("SC", d + [lastmul[0]],
                                         sc_act(tt1[:, :], ps_ap, AF.Silu,
                                                scale=s1_c[:, m:m + 1]))
                        return pend[m]
                    t2 = b.step("SC", d + [lastmul[0]],
                                sc_act(tt2[:, :], ps_ap, AF.Copy,
                                       scale=s2_c[:, m:m + 1]))

                    def fmul(v, m=m):
                        return v.tensor_mul(hb1_t[:, m * T:(m + 1) * T],
                                            tt1[:, :], tt2[:, :])
                    lastmul[0] = b.step("VE", [t2, pend[m]], fmul)
                    return lastmul[0]

                toks = mm_phase(CD, P, 2 * CH, wAT_b,
                                lambda ki: nb_t[:, ki * T:(ki + 1) * T],
                                evac_w12, [t_ln])

                def evac_h1(mo, ps_ap, d):
                    return b.step("SC", d, sc_act(
                        hb2_t[:, mo * T:(mo + 1) * T], ps_ap, AF.Gelu,
                        scale=h1s_c[:, mo:mo + 1]))
                toks = mm_phase(CH, P, CH, h1T_b,
                                lambda ki: hb1_t[:, ki * T:(ki + 1) * T],
                                evac_h1, toks[-8:])

                def evac_h2(mo, ps_ap, d):
                    return b.step("SC", d, sc_act(
                        hb1_t[:, mo * T:(mo + 1) * T], ps_ap, AF.Gelu,
                        scale=h2s_c[:, mo:mo + 1]))
                toks = mm_phase(CH, P, CH, h2T_b,
                                lambda ki: hb2_t[:, ki * T:(ki + 1) * T],
                                evac_h2, toks[-8:])

                def evac_w3(mo, ps_ap, d):
                    def f(v, mo=mo):
                        return v.scalar_tensor_tensor(
                            out=dst[:, mo * T:(mo + 1) * T], in0=ps_ap,
                            scalar=s3h_c[:, mo:mo + 1],
                            in1=x_src[:, mo * T:(mo + 1) * T],
                            op0=ALU.mult, op1=ALU.add)
                    return b.step("VE", d, f)
                toks = mm_phase(CH, P, CD, w3T_b,
                                lambda ki: hb1_t[:, ki * T:(ki + 1) * T],
                                evac_w3, toks[-8:])
                return toks

        def ph_mixer_layer(l, u_in, u_in_b, u_out, u_out_b, deps):
            layer_toks = list(deps)
            for hf in range(2):
                with ExitStack() as ms:
                    g_t = S_(ms, "g_t", (P, CI * TH))
                    dt_t = S_(ms, "dt_t", (P, CI * TH))
                    dtu_t = S_(ms, "dtu_t", (P, CI * TH))
                    y_t = S_(ms, "y_t", (P, CI * TH))
                    Bsb_t = S_(ms, "Bsb_t", (DS, TH), BF16)
                    Csb_t = S_(ms, "Csb_t", (DS, TH), BF16)

                    with ExitStack() as m1:
                        xin_t = S_(m1, "xin_t", (P, CI * (TH + 3)))
                        xcb_t = S_(m1, "xcb_t", (P, CI * TH), BF16)
                        dtrb_t = S_(m1, "dtrb_t", (DTR, TH), BF16)
                        dtp_l = S_(m1, "dtp_l", (DTR, DI), BF16)
                        sp0 = S_(m1, "sp0", (P, TH))
                        t_dtpw = b.step("SY", layer_toks,
                                        ld(dtp_l[:], dtpT_b[l]), n_dma=1,
                                        ch="dtp")

                        def evac_in(mo, ps_ap, d):
                            if mo < CI:
                                def f(s, mo=mo):
                                    return s.copy(
                                        out=xin_t[:, mo * (TH + 3) + 3:
                                                  mo * (TH + 3) + 3 + TH],
                                        in_=ps_ap)
                                return b.step("SC", d, f)
                            m = mo - CI
                            return b.step("SC", d, sc_act(
                                g_t[:, m * TH:(m + 1) * TH], ps_ap, AF.Silu))

                        def rhs_in(ki, hf=hf):
                            return u_in_b[:, ki * T + hf * TH:
                                          ki * T + hf * TH + TH]
                        toks_in = mm_phase(CD, P, 2 * CI, inT_b[l], rhs_in,
                                           evac_in, layer_toks, N=TH)

                        conv_toks = []
                        for ci in range(CI):
                            base = ci * (TH + 3)
                            tc3 = (l * CI + ci) * 3

                            def f_conv(v, ci=ci, base=base, tc3=tc3):
                                v.tensor_copy(out=xin_t[:, base:base + 3],
                                              in_=tails[:, tc3:tc3 + 3])
                                cw = lambda k: convw_c[
                                    :, (l * 4 + k) * CI + ci:
                                    (l * 4 + k) * CI + ci + 1]
                                v.tensor_scalar_mul(
                                    sp0[:, :], xin_t[:, base:base + TH], cw(0))
                                for k in (1, 2, 3):
                                    v.scalar_tensor_tensor(
                                        out=sp0[:, :],
                                        in0=xin_t[:, base + k:base + k + TH],
                                        scalar=cw(k), in1=sp0[:, :],
                                        op0=ALU.mult, op1=ALU.add)
                                return v.tensor_copy(
                                    out=tails[:, tc3:tc3 + 3],
                                    in_=xin_t[:, base + TH:base + TH + 3])
                            prev_silu = conv_toks[-1] if conv_toks else None
                            t_c = b.step("VE", [toks_in[ci], prev_silu], f_conv)
                            t_s = b.step("SC", [t_c], sc_act(
                                xcb_t[:, ci * TH:(ci + 1) * TH], sp0[:, :],
                                AF.Silu,
                                bias=convb_c[:, l * CI + ci:l * CI + ci + 1]))
                            conv_toks.append(t_s)
                        t_xcb = conv_toks[-1]

                        def f_xp(v, l=l):
                            for ki in range(CI):
                                v.matmul(ps[0][0:DTR, 0:TH],
                                         dtr_sb[:, (l * CI + ki) * DTR:
                                                (l * CI + ki + 1) * DTR],
                                         xcb_t[:, ki * TH:(ki + 1) * TH],
                                         start=(ki == 0), stop=(ki == CI - 1))
                            for ki in range(CI):
                                v.matmul(ps[1][0:DS, 0:TH],
                                         Bw_sb[:, (l * CI + ki) * DS:
                                               (l * CI + ki + 1) * DS],
                                         xcb_t[:, ki * TH:(ki + 1) * TH],
                                         start=(ki == 0), stop=(ki == CI - 1))
                            last = None
                            for ki in range(CI):
                                last = v.matmul(
                                    ps[2][0:DS, 0:TH],
                                    Cw_sb[:, (l * CI + ki) * DS:
                                          (l * CI + ki + 1) * DS],
                                    xcb_t[:, ki * TH:(ki + 1) * TH],
                                    start=(ki == 0), stop=(ki == CI - 1))
                            return last
                        t_xp = b.step("TE", [t_xcb] + toks_in[-8:], f_xp)
                        t_dtr = b.step("SC", [t_xp], sc_act(
                            dtrb_t[:], ps[0][0:DTR, 0:TH], AF.Copy))
                        t_Bs = b.step("SC", [t_xp], sc_act(
                            Bsb_t[:], ps[1][0:DS, 0:TH], AF.Copy))
                        t_Cs = b.step("SC", [t_xp], sc_act(
                            Csb_t[:], ps[2][0:DS, 0:TH], AF.Copy))

                        dtp_toks = []
                        for g0 in range(0, CI, 8):
                            def f_dtp(v, g0=g0, l=l):
                                last = None
                                for mi in range(8):
                                    mo = g0 + mi
                                    last = v.matmul(
                                        ps[mi][:, 0:TH],
                                        dtp_l[0:DTR, mo * P:(mo + 1) * P],
                                        dtrb_t[:], start=True, stop=True)
                                return last
                            dd = [t_dtr, t_Bs, t_Cs, t_dtpw] + dtp_toks[-8:]
                            t_mm = b.step("TE", dd, f_dtp)
                            for mi in range(8):
                                mo = g0 + mi
                                bcol = dtpb_c[:, l * CI + mo:l * CI + mo + 1]

                                def f_sp(s, mi=mi, mo=mo, bcol=bcol):
                                    s.activation(out=sp0[:, :],
                                                 in_=ps[mi][:, 0:TH],
                                                 func=AF.Abs, bias=bcol)
                                    s.activation(out=sp0[:, :], in_=sp0[:, :],
                                                 func=AF.Exp, scale=-1.0)
                                    s.activation(out=sp0[:, :], in_=sp0[:, :],
                                                 func=AF.Ln, bias=1.0)
                                    return s.activation(
                                        out=dt_t[:, mo * TH:(mo + 1) * TH],
                                        in_=ps[mi][:, 0:TH],
                                        func=AF.Relu, bias=bcol)
                                t_sp = b.step("SC", [t_mm] + dtp_toks[-1:],
                                              f_sp)

                                def f_dt(v, mo=mo):
                                    sl = slice(mo * TH, (mo + 1) * TH)
                                    return v.tensor_add(
                                        dt_t[:, sl], dt_t[:, sl], sp0[:, :])
                                dtp_toks.append(b.step("VE", [t_sp], f_dt))

                        def f_dtu(v):
                            last = None
                            for ci_ in range(CI):
                                sl = slice(ci_ * TH, (ci_ + 1) * TH)
                                v.tensor_mul(dtu_t[:, sl], dt_t[:, sl],
                                             xcb_t[:, sl])
                                last = v.tensor_scalar_mul(
                                    y_t[:, sl], xcb_t[:, sl],
                                    Dp_c[:, l * CI + ci_:l * CI + ci_ + 1])
                            return last
                        t_dtu = b.step("VE", dtp_toks + conv_toks[-1:], f_dtu)

                    with ExitStack() as m3:
                        Bb_t = S_(m3, "Bb_t", (P, DS * TH), BF16)
                        Cb_t = S_(m3, "Cb_t", (P, DS * TH), BF16)
                        ab_t = S_(m3, "ab_t", (P, NAB * TH))
                        bb_t = S_(m3, "bb_t", (P, NAB * TH))
                        yp_t = S_(m3, "yp_t", (P, NYP * TH))

                        bc_toks = []
                        for s_ in range(DS):
                            def f_mmb(v, s_=s_):
                                v.matmul(ps[4][:, 0:TH],
                                         selp_b[:, s_ * P:(s_ + 1) * P],
                                         Bsb_t[:], start=True, stop=True)
                                return v.matmul(ps[5][:, 0:TH],
                                                selp_b[:, s_ * P:(s_ + 1) * P],
                                                Csb_t[:], start=True, stop=True)
                            dd = [t_Bs, t_Cs, t_dtu] + (
                                bc_toks[-8:] if bc_toks else dtp_toks[-8:])
                            t_bm = b.step("TE", dd, f_mmb)
                            t_eb = b.step("SC", [t_bm], sc_act(
                                Bb_t[:, s_ * TH:(s_ + 1) * TH],
                                ps[4][:, 0:TH], AF.Copy))
                            t_ec = b.step("SC", [t_bm], sc_act(
                                Cb_t[:, s_ * TH:(s_ + 1) * TH],
                                ps[5][:, 0:TH], AF.Copy))
                            bc_toks += [t_eb, t_ec]

                        t_red_prev = None
                        y_toks = []
                        ab_toks = [None] * NAB
                        bb_toks = [None] * NAB
                        yp_toks = [None] * NYP
                        for ci_ in range(CI):
                            acol0 = (l * CI + ci_) * DS
                            a_done = []
                            b_done = []
                            for q0 in range(0, DS, NAB):
                                def f_a(s, ci_=ci_, q0=q0, acol0=acol0):
                                    last = None
                                    for qi in range(NAB):
                                        s_ = q0 + qi
                                        last = s.activation(
                                            out=ab_t[:, qi * TH:(qi + 1) * TH],
                                            in_=dt_t[:, ci_ * TH:
                                                     (ci_ + 1) * TH],
                                            func=AF.Exp,
                                            scale=A_cols[:, acol0 + s_:
                                                         acol0 + s_ + 1])
                                    return last
                                war_a = [ab_toks[q] for q in range(NAB)]
                                t_a = b.step("SC", [t_dtu] + war_a +
                                             bc_toks[-1:], f_a)

                                def f_b(g, ci_=ci_, q0=q0):
                                    last = None
                                    for qi in range(NAB):
                                        s_ = q0 + qi
                                        last = g.tensor_mul(
                                            bb_t[:, qi * TH:(qi + 1) * TH],
                                            dtu_t[:, ci_ * TH:(ci_ + 1) * TH],
                                            Bb_t[:, s_ * TH:(s_ + 1) * TH])
                                    return last
                                war_b = [bb_toks[q] for q in range(NAB)]
                                t_b = b.step("GP", [t_dtu] + war_b +
                                             bc_toks[-2:-1], f_b)

                                for qi in range(NAB):
                                    s_ = q0 + qi
                                    hcol = acol0 + s_
                                    yslot = s_ % NYP

                                    def f_scan(v, qi=qi, s_=s_, hcol=hcol,
                                               yslot=yslot):
                                        sl = slice(yslot * TH,
                                                   (yslot + 1) * TH)
                                        v.tensor_tensor_scan(
                                            out=yp_t[:, sl],
                                            data0=ab_t[:, qi * TH:
                                                       (qi + 1) * TH],
                                            data1=bb_t[:, qi * TH:
                                                       (qi + 1) * TH],
                                            initial=H_state[:, hcol:hcol + 1],
                                            op0=ALU.mult, op1=ALU.add)
                                        return v.tensor_copy(
                                            out=H_state[:, hcol:hcol + 1],
                                            in_=yp_t[:, yslot * TH + TH - 1:
                                                     yslot * TH + TH])
                                    t_sc = b.step(
                                        "VE", [t_a, t_b, yp_toks[yslot]],
                                        f_scan)
                                    ab_toks[qi] = t_sc
                                    bb_toks[qi] = t_sc
                                    if s_ < NYP:
                                        a_done.append(t_sc)
                                    else:
                                        b_done.append(t_sc)
                                    yp_toks[yslot] = t_sc

                                if q0 + NAB == NYP:
                                    def f_yc1(g, ci_=ci_):
                                        last = None
                                        for s_ in range(NYP):
                                            sl = slice(s_ * TH, (s_ + 1) * TH)
                                            last = g.tensor_mul(
                                                yp_t[:, sl], yp_t[:, sl],
                                                Cb_t[:, sl])
                                        return last
                                    t_gp1 = b.step("GP", a_done, f_yc1)

                                    def f_red1(v):
                                        return v.tensor_reduce(
                                            out=tt1[:, 0:TH],
                                            in_=yp_t[:].rearrange(
                                                "p (s t) -> p t s", s=NYP),
                                            axis=mybir.AxisListType.X,
                                            op=ALU.add)
                                    t_red1 = b.step("VE", [t_gp1], f_red1)
                                    for q in range(NYP):
                                        yp_toks[q] = t_red1

                            def f_yc2(g, ci_=ci_):
                                last = None
                                for s_ in range(NYP, DS):
                                    sl = slice((s_ - NYP) * TH,
                                               (s_ - NYP + 1) * TH)
                                    last = g.tensor_mul(
                                        yp_t[:, sl], yp_t[:, sl],
                                        Cb_t[:, s_ * TH:(s_ + 1) * TH])
                                return last
                            t_gp2 = b.step("GP", b_done, f_yc2)

                            def f_red2(v, ci_=ci_):
                                v.tensor_reduce(
                                    out=tt2[:, 0:TH],
                                    in_=yp_t[:].rearrange(
                                        "p (s t) -> p t s", s=NYP),
                                    axis=mybir.AxisListType.X, op=ALU.add)
                                sl = slice(ci_ * TH, (ci_ + 1) * TH)
                                v.tensor_add(tt1[:, 0:TH], tt1[:, 0:TH],
                                             tt2[:, 0:TH])
                                return v.tensor_add(y_t[:, sl], y_t[:, sl],
                                                    tt1[:, 0:TH])
                            t_red_prev = b.step("VE", [t_gp2], f_red2)
                            for q in range(NYP):
                                yp_toks[q] = t_red_prev
                            y_toks.append(t_red_prev)

                    with ExitStack() as m4:
                        y2b_t = S_(m4, "y2b_t", (P, CI * TH), BF16)
                        vb_t = S_(m4, "vb_t", (P, CD * TH), BF16)
                        hmb_t = S_(m4, "hmb_t", (P, (DI // P) * TH), BF16)
                        m2_t = S_(m4, "m2_t", (P, CD * TH))

                        def f_gate(g):
                            last = None
                            for ci_ in range(CI):
                                sl = slice(ci_ * TH, (ci_ + 1) * TH)
                                last = g.tensor_mul(y2b_t[:, sl], y_t[:, sl],
                                                    g_t[:, sl])
                            return last
                        t_gate = b.step("GP", y_toks, f_gate)

                        def evac_out(mo, ps_ap, d):
                            return b.step("SC", d, sc_act(
                                vb_t[:, mo * TH:(mo + 1) * TH], ps_ap,
                                AF.Copy))
                        toks_o = mm_phase(
                            CI, P, CD, outT_b[l],
                            lambda ki: y2b_t[:, ki * TH:(ki + 1) * TH],
                            evac_out, [t_gate], N=TH)

                        def evac_m1(mo, ps_ap, d):
                            return b.step("SC", d, sc_act(
                                hmb_t[:, mo * TH:(mo + 1) * TH], ps_ap,
                                AF.Gelu, bias=mlpb1_c[:, mo:mo + 1]))
                        toks_m1 = mm_phase(
                            CD, P, DI // P, mlp1T_b,
                            lambda ki: vb_t[:, ki * TH:(ki + 1) * TH],
                            evac_m1, toks_o[-8:], N=TH)

                        def evac_m2(mo, ps_ap, d):
                            return b.step("SC", d, sc_act(
                                m2_t[:, mo * TH:(mo + 1) * TH], ps_ap,
                                AF.Gelu, bias=mlpb2_c[:, mo:mo + 1]))
                        toks_m2 = mm_phase(
                            DI // P, P, CD, mlp2T_b,
                            lambda ki: hmb_t[:, ki * TH:(ki + 1) * TH],
                            evac_m2, toks_m1[-8:], N=TH)

                        t_rms = ph_rms(
                            lambda c: m2_t[:, c * TH:(c + 1) * TH],
                            lambda c, hf=hf: u_in[:, c * T + hf * TH:
                                                  c * T + hf * TH + TH],
                            lnw_c,
                            [lambda c, hf=hf: u_out[:, c * T + hf * TH:
                                                    c * T + hf * TH + TH],
                             lambda c, hf=hf: u_out_b[:, c * T + hf * TH:
                                                      c * T + hf * TH + TH]],
                            toks_m2, TH)
                        layer_toks = [t_rms]
            return layer_toks

        toks = list(start_toks)
        dbg_requests = set(debug_taps)
        dbg_tensors = {}

        def dbg(name, j, ap, deps):
            if (name, j) in dbg_requests:
                o = nc.dram_tensor(f"dbg_{name}_{j}", list(ap.shape), F32,
                                   kind="ExternalOutput").ap()
                return [b.step("SY", deps, ld(o, ap), n_dma=1, ch="dbg")]
            return []

        t_store_last = None
        for j in range(n_tiles):
            b.seg = f"t{j}"
            t_x = b.step("SY", toks, ld(
                x_t[:].rearrange("p (c t) -> p c t", t=T),
                x_d.rearrange("p (c t) -> p c t", t=L)[:, :, j * T:(j + 1) * T]
                ), n_dma=1, ch="xl")
            toks = ph_ffn(x_t, x1_t, toks + [t_x])
            toks += dbg("x1", j, x1_t[:], toks)
            t_u0 = ph_ln(x1_t, u0_t, None, toks)
            t_u0b = b.step("GP", [t_u0], lambda g: g.tensor_copy(
                out=nb_t[:], in_=u0_t[:]))
            lt = ph_mixer_layer(0, u0_t, nb_t, ua_t, nb2_t, [t_u0, t_u0b])
            toks = lt + dbg("u1", j, ua_t[:], lt)
            lt = ph_mixer_layer(1, ua_t, nb2_t, x_t, nb_t, toks)
            toks = lt + dbg("u2", j, x_t[:], lt)
            t_mo = ph_rms(lambda c: x_t[:, c * T:(c + 1) * T],
                          lambda c: u0_t[:, c * T:(c + 1) * T],
                          ln2w_c,
                          [lambda c: u0_t[:, c * T:(c + 1) * T]],
                          toks, T)

            def f_x2(v):
                last = None
                for c in range(CD):
                    sl = slice(c * T, (c + 1) * T)
                    last = v.tensor_add(x1_t[:, sl], x1_t[:, sl], u0_t[:, sl])
                return last
            t_x2 = b.step("VE", [t_mo], f_x2)
            toks = [t_x2] + dbg("x2", j, x1_t[:], [t_x2])
            toks = ph_ffn(x1_t, x_t, toks)
            t_store_last = b.step("SY", toks, ld(
                out_d.rearrange("p (c t) -> p c t", t=L)[:, :, j * T:(j + 1) * T],
                x_t[:].rearrange("p (c t) -> p c t", t=T)), n_dma=1, ch="st")
            toks = toks + [t_store_last]

        b.step("GP", [t_store_last], lambda g: g.memset(tt1[0:1, 0:1], 0.0),
               inc=False)

        _emit(nc, b, es)
    return nc


def _selp_const():
    sp = np.zeros((DS, DS * P), np.float32)
    for s in range(DS):
        sp[s, s * P:(s + 1) * P] = 1.0
    return sp


def _relay(inputs):
    f = np.ascontiguousarray
    xp = inputs["L0_attn_x_proj_w"]
    w1T = inputs["L0_ffn_w1_w"].T
    w2T = inputs["L0_ffn_w2_w"].T
    wAT = np.stack([w1T.reshape(D, CH, P), w2T.reshape(D, CH, P)],
                   axis=2).reshape(D, 2 * DH)
    common = {
        "wAT": f(wAT),
        "h1T": f(inputs["L0_ffn_h1_w"].T),
        "h2T": f(inputs["L0_ffn_h2_w"].T),
        "w3T": f(inputs["L0_ffn_w3_w"].T),
        "inT": f(inputs["L0_attn_in_proj_w"].transpose(0, 2, 1)),
        "outT": f(inputs["L0_attn_out_proj_w"].transpose(0, 2, 1)),
        "mlp1T": f(inputs["L0_attn_mlp_w1"].T),
        "mlp2T": f(inputs["L0_attn_mlp_w2"].T),
        "dtpT": f(inputs["L0_attn_dt_proj_w"].transpose(0, 2, 1)),
        "dtrT": f(xp[:, :DTR, :].transpose(0, 2, 1)),
        "BT": f(xp[:, DTR:DTR + DS, :].transpose(0, 2, 1)),
        "CT": f(xp[:, DTR + DS:, :].transpose(0, 2, 1)),
        "alog": f(inputs["L0_attn_A_log"].reshape(2, CI, P, DS)
                  .transpose(2, 0, 1, 3).reshape(P, 2 * CI * DS)),
        "convw": f(inputs["L0_attn_conv_w"].reshape(2, CI, P, 4)
                   .transpose(2, 0, 3, 1).reshape(P, 2 * 4 * CI)),
        "convb": f(inputs["L0_attn_conv_b"].reshape(2, CI, P)
                   .transpose(2, 0, 1).reshape(P, 2 * CI)),
        "dtpb": f(inputs["L0_attn_dt_proj_b"].reshape(2, CI, P)
                  .transpose(2, 0, 1).reshape(P, 2 * CI)),
        "Dp": f(inputs["L0_attn_D_param"].reshape(2, CI, P)
                .transpose(2, 0, 1).reshape(P, 2 * CI)),
        "mlpb1": f(inputs["L0_attn_mlp_b1"].reshape(DI // P, P).T),
        "mlpb2": f(inputs["L0_attn_mlp_b2"].reshape(CD, P).T),
        "lnw": f(inputs["L0_attn_ln_w"].reshape(CD, P).T),
        "ln2w": f(inputs["L0_attn_ln2_w"].reshape(CD, P).T),
        "s1": f(inputs["L0_ffn_w1_s"].reshape(CH, P).T),
        "s2": f(inputs["L0_ffn_w2_s"].reshape(CH, P).T),
        "h1s": f(inputs["L0_ffn_h1_s"].reshape(CH, P).T),
        "h2s": f(inputs["L0_ffn_h2_s"].reshape(CH, P).T),
        "s3": f(inputs["L0_ffn_w3_s"].reshape(CD, P).T),
        "selp": _selp_const(),
    }
    return {k: np.asarray(v, np.float32) for k, v in common.items()}


def kernel(**inputs):
    inputs = {k: np.asarray(v) for k, v in inputs.items()}
    common = _relay(inputs)
    x = inputs["L0_attn_x"]
    n_cores = 8
    in_maps = []
    for c in range(n_cores):
        m = dict(common)
        xc_ = x[c].T.reshape(CD, P, L).transpose(1, 0, 2).reshape(P, CD * L)
        m["x"] = np.ascontiguousarray(xc_).astype(np.float32)
        in_maps.append(m)
    nc = build_program()
    res = run_bass_kernel_spmd(nc, in_maps, core_ids=list(range(n_cores)))
    outs = []
    for c in range(n_cores):
        o = res.results[c]["out"].reshape(P, CD, L).transpose(1, 0, 2)
        outs.append(o.reshape(D, L).T)
    return np.ascontiguousarray(np.stack(outs, 0)).astype(np.float32)


if __name__ == "__main__":
    import reference as R
    inp = R.setup_inputs(0)
    out = kernel(**{k: np.asarray(v) for k, v in inp.items()})
    print(out.shape, out.dtype)
